# version 01 (final)
# baseline (speedup 1.0000x reference)
import math
import numpy as np
from contextlib import ExitStack

import concourse.bass as bass
import concourse.mybir as mybir
from concourse.bass_utils import run_bass_kernel_spmd

F32 = mybir.dt.float32
BF16 = mybir.dt.bfloat16
I8 = mybir.dt.int8
I32 = mybir.dt.int32
AF = mybir.ActivationFunctionType
ALU = mybir.AluOpType
AX = mybir.AxisListType

TOK = 1024
S = 2048
H = 2048
INTER = 8192
EPS = 1e-5
SCL = 1.0 / math.sqrt(128.0)
TWO_PI = 6.283185307179586
PI_HI = 6.28125
PI_LO = TWO_PI - PI_HI
LNT = math.log(10000.0)
MCAP = 80.0
MSHIFT = 45.0
NQT = TOK // 128


class Prog:
    ENGS = ("tensor", "vector", "scalar", "gpsimd", "sync")
    DMA_SEMS = frozenset(("gA", "gB", "sA", "sB", "io", "bA", "bB", "xA",
                          "xB", "fA", "fB", "vA", "st", "oA", "oB", "oC",
                          "oD"))

    def __init__(self):
        self.sched = {e: [] for e in self.ENGS}
        self.cnt = {}
        self.handles = {}

    def newsem(self, name):
        self.cnt[name] = 0

    SELF_SEM = {"vector": "ve", "scalar": "se", "gpsimd": "gp"}

    def op(self, eng, fn, inc=None, n=1):
        ssem = self.SELF_SEM.get(eng)
        if ssem is not None and inc == ssem:
            cur = self.cnt[ssem]
            if cur > 0:
                def runw(e, ssem=ssem, cur=cur):
                    e.wait_ge(self.handles[ssem], cur)
                self.sched[eng].append(runw)
        if inc is not None:
            self.cnt[inc] += n

            def run(e, fn=fn, inc=inc, n=n):
                fn(e).then_inc(self.handles[inc], n)
        else:
            def run(e, fn=fn):
                fn(e)
        self.sched[eng].append(run)
        return self.cnt[inc] if inc else None

    def wait(self, eng, sem, val):
        if val is None or val <= 0:
            return
        if sem in self.DMA_SEMS:
            val = self.cnt[sem]

        def run(e, sem=sem, val=val):
            e.wait_ge(self.handles[sem], val)
        self.sched[eng].append(run)

    def mark(self, sem):
        return self.cnt[sem]

    def barrier(self):
        for sem in sorted(self.DMA_SEMS):
            if self.cnt[sem] > 0:
                self.wait("sync", sem, self.cnt[sem])
        marks = {}
        for e in self.ENGS:
            nm = "b_" + e
            self.cnt[nm] += 1
            drain = e in ("tensor", "vector", "scalar")

            def run(E, nm=nm, drain=drain):
                if drain:
                    E.drain()
                E.sem_inc(self.handles[nm], 1)
            self.sched[e].append(run)
            marks[e] = self.cnt[nm]
        for e in self.ENGS:
            for o in self.ENGS:
                if o != e:
                    self.wait(e, "b_" + o, marks[o])


def _tile_cols(w, ncol):
    K, N = w.shape
    return np.ascontiguousarray(
        w.reshape(K // 128, 128, N // ncol, ncol).transpose(2, 1, 0, 3))


def stage_inputs(i):
    x = np.asarray(i["L0_attn_hidden_states"], dtype=np.float32)
    wq = np.asarray(i["L0_attn_Wq"], np.float32)
    wk = np.asarray(i["L0_attn_Wk"], np.float32)
    wv = np.asarray(i["L0_attn_Wv"], np.float32)
    wo = np.asarray(i["L0_attn_Wo"], np.float32)

    def dstrip(w):
        return np.ascontiguousarray(
            np.asarray(w).reshape(64, 128, 8, 256).transpose(2, 1, 0, 3))

    shared = dict(
        wkT=_tile_cols(np.ascontiguousarray(wk.T), 512)[0],
        wvT=_tile_cols(np.ascontiguousarray(wv.T), 512)[0],
        wqT=_tile_cols(np.ascontiguousarray(wq.T), 512),
        woT=_tile_cols(np.ascontiguousarray(wo.T), 512),
        gw0=_tile_cols(np.asarray(i["L0_ffn_w_gate"]), 512),
        uw0=_tile_cols(np.asarray(i["L0_ffn_w_up"]), 512),
        dw0=dstrip(i["L0_ffn_w_down"]),
        sg0=np.asarray(i["L0_ffn_s_gate"], np.float32),
        su0=np.asarray(i["L0_ffn_s_up"], np.float32),
        sd0=np.asarray(i["L0_ffn_s_down"], np.float32),
        caw=_tile_cols(np.asarray(i["L1_attn_c_attn_w"], np.float32), 512),
        cab=np.asarray(i["L1_attn_c_attn_b"], np.float32),
        cabt=np.ascontiguousarray(
            np.asarray(i["L1_attn_c_attn_b"], np.float32)[0:4096]
            .reshape(32, 128).T),
        cpw=_tile_cols(np.asarray(i["L1_attn_c_proj_w"], np.float32), 512),
        cpb=np.asarray(i["L1_attn_c_proj_b"], np.float32),
        cgate=np.asarray(i["L1_attn_cam_gate"], np.float32),
        cw0=np.asarray(i["L1_attn_cam_w0"], np.float32),
        cw1=np.asarray(i["L1_attn_cam_w1"], np.float32),
        gw1=_tile_cols(np.asarray(i["L1_ffn_w_gate"], np.float32), 512),
        uw1=_tile_cols(np.asarray(i["L1_ffn_w_up"], np.float32), 512),
        dw1=dstrip(np.asarray(i["L1_ffn_w_down"], np.float32)),
    )

    maps = []
    for c in range(8):
        s_, h_ = c // 2, c % 2
        xt = np.ascontiguousarray(
            x[s_].T.reshape(16, 128, 2048).transpose(1, 0, 2))
        d = dict(shared)
        d["x_own"] = np.ascontiguousarray(x[s_, h_ * TOK:(h_ + 1) * TOK, :])
        d["xfm_own"] = np.ascontiguousarray(xt[:, :, h_ * TOK:(h_ + 1) * TOK])
        d["xfmA"] = np.ascontiguousarray(
            xt[:, :, 0:TOK].reshape(128, 16, 8, 128).transpose(2, 0, 1, 3))
        d["xfmB"] = np.ascontiguousarray(
            xt[:, :, TOK:S].reshape(128, 16, 8, 128).transpose(2, 0, 1, 3))
        d["pos0"] = np.array([h_ * TOK], np.float32)
        d["mshift"] = np.array([-MSHIFT, 0, 0, 0], np.float32)
        maps.append(d)
    return maps


def build_nc(taps=(), upto=99, skip=()):
    nc = bass.Bass("TRN2", target_bir_lowering=False, debug=False)
    P = Prog()
    for s_ in ("gA", "gB", "sA", "sB", "io", "bA", "bB", "xA", "xB",
               "fA", "fB", "vA", "st", "oA", "oB", "oC", "oD",
               "te", "ve", "se", "cc", "gp"):
        P.newsem(s_)
    for en in Prog.ENGS:
        P.newsem("b_" + en)

    def din(name, shape, dt=F32):
        return nc.dram_tensor(name, list(shape), dt, kind="ExternalInput").ap()

    x_own = din("x_own", [TOK, H])
    xfm_own = din("xfm_own", [128, 16, TOK])
    xfmA = din("xfmA", [8, 128, 16, 128])
    xfmB = din("xfmB", [8, 128, 16, 128])
    wkT = din("wkT", [128, 16, 512])
    wvT = din("wvT", [128, 16, 512])
    wqT = din("wqT", [4, 128, 16, 512])
    woT = din("woT", [4, 128, 16, 512])
    gw0 = din("gw0", [16, 128, 16, 512], I8)
    uw0 = din("uw0", [16, 128, 16, 512], I8)
    dw0 = din("dw0", [8, 128, 64, 256], I8)
    sg0 = din("sg0", [INTER])
    su0 = din("su0", [INTER])
    sd0 = din("sd0", [H])
    caw = din("caw", [12, 128, 16, 512])
    cab = din("cab", [3 * H])
    cabt = din("cabt", [128, 32])
    cpw = din("cpw", [4, 128, 16, 512])
    cpb = din("cpb", [H])
    cgate = din("cgate", [128])
    cw0 = din("cw0", [1])
    cw1 = din("cw1", [1])
    gw1 = din("gw1", [16, 128, 16, 512])
    uw1 = din("uw1", [16, 128, 16, 512])
    dw1 = din("dw1", [8, 128, 64, 256])
    pos0 = din("pos0", [1])
    mshift = din("mshift", [4])

    y = nc.dram_tensor("y", [TOK, H], F32, kind="ExternalOutput").ap()

    xm = nc.dram_tensor("xm", [TOK, H], F32).ap()
    sumb = nc.dram_tensor("sumb", [TOK, H], F32).ap()
    fT = nc.dram_tensor("fT", [64, 128, 8, 128], BF16).ap()
    colrt = nc.dram_tensor("colrt", [256], F32).ap()
    mrow = nc.dram_tensor("mrow", [64], F32).ap()
    kag_i = nc.dram_tensor("kag_i", [8, 128, 16, 128], BF16).ap()
    kag_o = nc.dram_tensor("kag_o", [2, 8, 128, 16, 128], BF16).ap()
    vag_i = nc.dram_tensor("vag_i", [8, 128, 4, 512], BF16).ap()
    vag_o = nc.dram_tensor("vag_o", [2, 8, 128, 4, 512], BF16).ap()
    sag_i = nc.dram_tensor("sag_i", [16], F32).ap()
    sag_o = nc.dram_tensor("sag_o", [2, 16], F32).ap()

    tap_outs = {}
    for t in taps:
        shp = {"sumb0": [TOK, H], "x1": [TOK, H], "sumb1": [TOK, H],
               "x2": [TOK, H], "sumb2": [TOK, H], "x3": [TOK, H],
               "qrot": [128, 16, TOK], "krot": [128, 4, S],
               "vaug": [128, 16, 520], "mstat": [128, 128],
               "tabs": [128, 4096], "tabs0": [128, 128],
               "bsc": [128, 128], "ctx": [128, 16, 512],
               "kcam": [128, 16, 128], "qcam": [128, TOK],
               "vcam": [128, 16, 128], "camT": [128, 512],
               "ctx1": [128, 16, 512], "camf": [128, 384]}[t]
        dt = BF16 if t in ("qrot", "krot", "vaug", "bsc", "ctx",
                           "kcam", "qcam", "vcam", "camT", "ctx1") else F32
        tap_outs[t] = nc.dram_tensor("tap_" + t, shp, dt,
                                     kind="ExternalOutput").ap()

    rep_groups = [[0, 1], [2, 3], [4, 5], [6, 7]]

    with ExitStack() as ctx:
        e = ctx.enter_context
        ABF = e(nc.sbuf_tensor("ABF", [128, 86144], BF16))
        AF32 = e(nc.sbuf_tensor("AF32", [128, 8192], F32))
        COLS = e(nc.sbuf_tensor("COLS", [128, 128], F32))
        EYE = e(nc.sbuf_tensor("EYE", [128, 128], BF16))
        ONES = e(nc.sbuf_tensor("ONES", [128, 16], BF16))
        IC = e(nc.sbuf_tensor("IC", [128, 512], I32))
        ROWSCR = e(nc.sbuf_tensor("ROWSCR", [128, 128], F32))
        BSC = e(nc.sbuf_tensor("BSC", [128, 128], BF16))
        CAB2 = e(nc.sbuf_tensor("CAB2", [128, 32], F32))

        PS = [e(nc.psum_tensor(f"PS{k}", [128, 512], F32)) for k in range(6)]
        psT2 = [e(nc.psum_tensor("psTa", [128, 128], BF16)),
                e(nc.psum_tensor("psTb", [128, 128], BF16))]

        for s_ in list(P.cnt):
            P.handles[s_] = e(nc.semaphore("sem_" + s_))

        block = e(nc.Block())

        def view(tensor, off, shape):
            n = int(np.prod(shape))
            ap = tensor.ap()[:, off:off + n]
            if len(shape) == 1:
                return ap
            pat = "p (" + " ".join(f"d{k}" for k in range(len(shape))) + ") -> p " \
                  + " ".join(f"d{k}" for k in range(len(shape)))
            return ap.rearrange(pat, **{f"d{k}": shape[k] for k in range(len(shape))})

        O_XT, O_QT, O_WA, O_WB = 0, 16384, 32768, 40960
        O_KT, O_VB, O_PT, O_PT2, O_SM = 49152, 57344, 65664, 73856, 82048
        XT = view(ABF, O_XT, (16, TOK))
        QT = view(ABF, O_QT, (16, TOK))
        WA = view(ABF, O_WA, (16, 512))
        WB = view(ABF, O_WB, (16, 512))
        KT = view(ABF, O_KT, (4, 2048))
        VB = view(ABF, O_VB, (16, 520))
        VBS = view(ABF, O_VB, (16, 512))
        VCAM = view(ABF, O_WB, (16, 130))
        PT = view(ABF, O_PT, (16, 512))
        PT2 = view(ABF, O_PT2, (16, 512))
        CASTB = view(ABF, O_PT, (2048,))
        WPT = view(ABF, O_PT, (16, 512))
        WXT = view(ABF, O_XT, (16, 512))
        QTa = view(ABF, O_QT, (16, 512))
        QTb = view(ABF, O_QT + 8192, (16, 512))
        STRIP0 = view(ABF, O_QT, (64, 256))
        STRIP1 = view(ABF, O_WA, (64, 256))
        FTS = view(ABF, O_KT, (8, 8, 128))
        KCAM = view(ABF, O_KT, (16, 128))
        KTS = [view(ABF, O_KT + 2048, (16, 128)),
               view(ABF, O_KT + 4096, (16, 128))]
        FTW = view(ABF, O_VB, (4, 8, 128))
        SM_XTP = [view(ABF, O_SM, (16, 128)), view(ABF, O_SM + 2048, (16, 128))]
        ROPE_ST = [view(ABF, O_PT2, (4, 128)), view(ABF, O_PT2 + 512, (4, 128))]
        SM_STG = view(ABF, O_SM, (512,))
        SM_STG4 = view(ABF, O_SM, (4, 128))
        SM_T = view(ABF, O_SM + 1024, (128,))
        SM_TT = view(ABF, O_SM + 1152, (512,))
        SM_C6 = [view(ABF, O_SM + 1664 + 128 * k, (128,)) for k in range(6)]
        SM_K4 = [view(ABF, O_SM + 2432 + 128 * k, (128,)) for k in range(4)]
        SM_TMP = view(ABF, O_SM + 2944, (128,))
        FUSEB = view(ABF, O_PT2, (512,))
        FUSE4 = view(ABF, O_PT2, (128, 4))

        FS = view(AF32, 0, (2048,))
        FS2 = view(AF32, 2048, (2048,))
        FS3 = view(AF32, 4096, (1024,))
        FAR = view(AF32, 5120, (2048,))
        COSK = view(AF32, 5120, (16, 64))
        SINK = view(AF32, 6144, (16, 64))
        COSQ = view(AF32, 7168, (8, 64))
        SINQ = view(AF32, 7680, (8, 64))
        THQ = view(AF32, 0, (8, 64))
        THK = view(AF32, 512, (16, 64))
        IVF = view(AF32, 4096, (64,))
        RSC = view(AF32, 2048 + 1024, (1024,))

        def col(k):
            return COLS.ap()[:, 4 * k:4 * k + 1]
        C_QST, C_KST, C_M0, C_RC, C_MEAN, C_RSTD = 0, 1, 2, 3, 4, 5
        C_SUM, C_SQ, C_VAR, C_T1, C_T2, C_T3 = 6, 7, 8, 9, 10, 11
        C_KCST, C_M1, C_MC, C_LW, C_POS0, C_QCST = 12, 13, 14, 15, 16, 17
        C_RC4 = 120
        CABV = CAB2.ap()[:, 0:32]

        GP_SEMS = ("gA", "gB", "sA", "sB")

        def dma(qsem, out, in_, slow=False):
            eng = "gpsimd" if qsem in GP_SEMS else "sync"
            if slow:
                return P.op(eng, lambda E, o=out, i=in_: E.dma_start(
                    out=o, in_=i, allow_slow_non_contiguous=True),
                    inc=qsem, n=16)
            return P.op(eng, lambda E, o=out, i=in_: E.dma_start(out=o, in_=i),
                        inc=qsem, n=16)

        def bcast_row(dst, src_elem_ap, n, sem="io"):
            t, off = src_elem_ap.tensor, src_elem_ap.offset
            src = bass.AP(t, off, [[0, 128], [1, n]])
            return dma(sem, dst, src)

        def ve(fn):
            return P.op("vector", fn, inc="ve")

        def se(fn):
            return P.op("scalar", fn, inc="se")

        def te(fn):
            return P.op("tensor", fn, inc="te")

        psT_ev = [0, 0]

        def transpose_to(dst, src, src_ve=None):
            par = transpose_to.par
            transpose_to.par ^= 1
            half = psT2[par].ap()[:, :]
            P.wait("tensor", "ve", max(psT_ev[par],
                                       P.mark("ve") if src_ve is None else src_ve))
            m_t = te(lambda E, h=half, s=src: E.transpose(h, s, EYE[:, :]))
            P.wait("vector", "te", m_t)
            m_v = ve(lambda E, d=dst, h=half: E.tensor_copy(d, h))
            psT_ev[par] = m_v
            return m_t, m_v
        transpose_to.par = 0

        def colmax_allparts(cols_ap, ncol, dst_col):
            if ncol == 1:
                ve(lambda E, a=cols_ap: E.tensor_copy(col(C_T3), a))
            else:
                ve(lambda E, a=cols_ap: E.tensor_reduce(
                    col(C_T3), a, AX.X, ALU.max))
            ve(lambda E: E.memset(ROWSCR.ap()[:, :], 1.0))
            mv = ve(lambda E: E.tensor_scalar(
                BSC.ap()[:, :], ROWSCR.ap()[:, :], col(C_T3), None, ALU.mult))
            transpose_to(BSC.ap()[:, 0:128], BSC.ap()[:, 0:128], src_ve=mv)
            return ve(lambda E, dc=dst_col: E.reduce_max(
                col(dc), BSC.ap()[:, 0:128], AX.X))

        def colb(c_idx, n):
            cap = col(c_idx)
            return bass.AP(cap.tensor, cap.offset,
                           [[cap.ap[0][0], 128], [0, n]])

        def col_const(dst_col, v):
            return ve(lambda E, d=dst_col, v=v: E.memset(col(d), v))

        def col_tt(dst_col, a_col, b_col, op):
            return ve(lambda E, d=dst_col, a=a_col, b=b_col, op=op:
                      E.tensor_tensor(col(d), col(a), col(b), op))

        def m_chain(src_col, scale, dst_col, tmp=C_SQ):
            col_tt(src_col, src_col, src_col, ALU.mult)
            col_const(tmp, scale)
            col_tt(src_col, src_col, tmp, ALU.mult)
            col_const(tmp, MCAP)
            col_tt(src_col, src_col, tmp, ALU.min)
            col_const(tmp, 0.0)
            col_tt(src_col, src_col, tmp, ALU.max)
            col_const(tmp, -1.0)
            return col_tt(dst_col, src_col, tmp, ALU.mult)


        P.op("gpsimd", lambda E: E.iota(EYE[:, :], [[-1, 128]], base=0,
                                        channel_multiplier=1,
                                        allow_small_or_imprecise_dtypes=True),
             inc="gp")
        P.op("gpsimd", lambda E: E.iota(IVF, [[1, 64]], base=0,
                                        channel_multiplier=0,
                                        allow_small_or_imprecise_dtypes=True),
             inc="gp")
        P.op("gpsimd", lambda E: E.iota(THK, [[128, 16], [0, 64]], base=0,
                                        channel_multiplier=1,
                                        allow_small_or_imprecise_dtypes=True),
             inc="gp")
        P.op("gpsimd", lambda E: E.iota(THQ, [[128, 8], [0, 64]], base=0,
                                        channel_multiplier=1,
                                        allow_small_or_imprecise_dtypes=True),
             inc="gp")
        m_gp = P.mark("gp")
        P.wait("vector", "gp", m_gp)
        ve(lambda E: E.tensor_single_scalar(EYE[:, :], EYE[:, :], 0.0,
                                            ALU.is_equal))
        ve(lambda E: E.memset(ONES[:, :], 1.0))
        m_cz = ve(lambda E: E.memset(COLS.ap()[:, :], 0.0))
        P.wait("sync", "ve", m_cz)
        m_p0 = bcast_row(col(C_POS0), pos0[0:1], 1)
        m_m0b = bcast_row(col(C_M0), mshift[0:1], 1)
        m_m0b = dma("io", CABV, cabt[:, :])
        P.wait("scalar", "gp", m_gp)
        m_ivf = se(lambda E: E.activation(out=IVF, in_=IVF, func=AF.Exp,
                                          bias=0.0, scale=-LNT / 64.0))
        P.wait("vector", "io", m_p0)
        P.wait("vector", "se", m_ivf)
        ve(lambda E: E.scalar_tensor_tensor(
            THQ, THQ, 1.0,
            bass.AP(COLS.ap().tensor, COLS.ap().offset + 4 * C_POS0,
                    [[COLS.ap().ap[0][0], 128], [0, 8], [0, 64]]),
            ALU.mult, ALU.add))

        def ivf_b(nt):
            return bass.AP(IVF.tensor, IVF.offset,
                           [[IVF.ap[0][0], 128], [0, nt], [1, 64]])
        ve(lambda E: E.tensor_tensor(THK, THK, ivf_b(16), ALU.mult))
        ve(lambda E: E.tensor_tensor(THQ, THQ, ivf_b(8), ALU.mult))

        def sin_reduced(dst, src, n, extra_bias):
            mm = 0
            for o in range(0, n, 512):
                w = min(512, n - o)
                d, s2 = dst[:, o:o + w], src[:, o:o + w]
                tmp = RSC[:, 0:w]
                icv = IC.ap()[:, 0:w]
                if extra_bias != 0.0:
                    ve(lambda E, t=tmp, s=s2, b=extra_bias:
                       E.tensor_scalar_add(t, s, b))
                    sx = tmp
                    tmp2 = RSC[:, 512:512 + w]
                else:
                    sx = s2
                    tmp2 = tmp
                ve(lambda E, d=d, s=sx: E.tensor_scalar_mul(d, s, 1.0 / TWO_PI))
                ve(lambda E, i=icv, d=d: E.tensor_copy(i, d))
                ve(lambda E, d=d, i=icv: E.tensor_copy(d, i))
                ve(lambda E, d=d, s=sx: E.scalar_tensor_tensor(
                    d, d, -PI_HI, s, ALU.mult, ALU.add))
                ve(lambda E, t=tmp2, i=icv: E.tensor_copy(t, i))
                ve(lambda E, d=d, t=tmp2: E.scalar_tensor_tensor(
                    d, t, -PI_LO, d, ALU.mult, ALU.add))
                ve(lambda E, d=d, t=tmp2: E.tensor_single_scalar(
                    t, d, math.pi, ALU.is_gt))
                ve(lambda E, d=d, t=tmp2: E.scalar_tensor_tensor(
                    d, t, -PI_HI, d, ALU.mult, ALU.add))
                m0 = ve(lambda E, d=d, t=tmp2: E.scalar_tensor_tensor(
                    d, t, -PI_LO, d, ALU.mult, ALU.add))
                P.wait("scalar", "ve", m0)
                mm = se(lambda E, d=d: E.activation(out=d, in_=d, func=AF.Sin))
                P.wait("vector", "se", mm)
            return mm

        THKf = THK.rearrange("p a b -> p (a b)")
        THQf = THQ.rearrange("p a b -> p (a b)")
        SINKf = SINK.rearrange("p a b -> p (a b)")
        COSKf = COSK.rearrange("p a b -> p (a b)")
        SINQf = SINQ.rearrange("p a b -> p (a b)")
        COSQf = COSQ.rearrange("p a b -> p (a b)")
        sin_reduced(SINKf, THKf, 1024, 0.0)
        sin_reduced(COSKf, THKf, 1024, math.pi / 2.0)
        sin_reduced(SINQf, THQf, 512, 0.0)
        mse = sin_reduced(COSQf, THQf, 512, math.pi / 2.0)
        P.wait("vector", "se", mse)
        ve(lambda E: E.tensor_scalar_mul(SINQf, SINQf, SCL))
        ve(lambda E: E.tensor_scalar_mul(COSQf, COSQf, SCL))

        dma("gA", XT[:, :, :], xfm_own[:, :, :])
        ve(lambda E: E.memset(view(ABF, O_VB, (8320,)), 0.0))
        for g in range(4):
            ve(lambda E, g=g: E.memset(VB[:, :, g * 130 + 128:g * 130 + 129], 1.0))

        P.barrier()
        if upto < 1:
            _emit(nc, P, block)
            return nc

        dma("gA", WA[:, :, :], wkT[:, :, :])
        m_wkv = dma("gA", WB[:, :, :], wvT[:, :, :])

        slab_te = [0, 0]
        rope_te = [0, 0]
        kv_ps_ve = [0, 0, 0, 0]

        for kt in range(16) if "kvloop" not in skip else []:
            par = kt % 2
            slab = SM_XTP[par]
            src = (xfmA if kt < 8 else xfmB)[kt % 8]
            P.wait("gpsimd", "te", slab_te[par])
            ssem = "sA" if par == 0 else "sB"
            m_slab = dma(ssem, slab[:, :, :], src)
            P.wait("tensor", ssem, m_slab)
            if kt == 0:
                P.wait("tensor", "gA", m_wkv)
            pk, pv = PS[par], PS[2 + par]
            P.wait("tensor", "ve", kv_ps_ve[par])
            for ec in range(16):
                te(lambda E, p=pk, s=slab, ec=ec: E.matmul(
                    p[:, :], s[:, ec, :], WA[:, ec, :],
                    start=(ec == 0), stop=(ec == 15)))
            P.wait("tensor", "ve", kv_ps_ve[2 + par])
            for ec in range(16):
                mm = te(lambda E, p=pv, s=slab, ec=ec: E.matmul(
                    p[:, :], s[:, ec, :], WB[:, ec, :],
                    start=(ec == 0), stop=(ec == 15)))
            slab_te[par] = mm
            P.wait("vector", "te", mm)
            VBk = VB[:, kt, :].rearrange("p (g c) -> p g c", c=130)
            PVv = pv.ap().rearrange("p (g c) -> p g c", c=128)
            kv_ps_ve[2 + par] = ve(
                lambda E, o=VBk, i=PVv: E.tensor_copy(o[:, :, 0:128], i))
            kv_ps_ve[par] = ve(
                lambda E, p=pk: E.tensor_copy(FS[:, 0:512], p[:, :]))
            FSg = FS[:, 0:512].rearrange("p (g c) -> p g c", c=128)

            def ckb(tab, kt=kt):
                sl = tab[:, kt, :]
                return bass.AP(sl.tensor, sl.offset,
                               [[sl.ap[0][0], 128], [0, 4], [1, 64]])
            T1 = FS2[:, 0:512].rearrange("p (g c) -> p g c", c=64)
            T1a, T1b = T1[:, 0:4, :], T1[:, 4:8, :]
            RO = ROPE_ST[par]
            P.wait("vector", "te", rope_te[par])
            ve(lambda E, t=T1a, f=FSg, kt=kt: E.tensor_tensor(
                t, f[:, :, 0:64], ckb(COSK, kt), ALU.mult))
            ve(lambda E, t=T1b, f=FSg, kt=kt: E.tensor_tensor(
                t, f[:, :, 64:128], ckb(SINK, kt), ALU.mult))
            ve(lambda E, o=RO, a=T1a, b=T1b: E.tensor_tensor(
                o[:, :, 0:64], a, b, ALU.subtract))
            ve(lambda E, t=T1a, f=FSg, kt=kt: E.tensor_tensor(
                t, f[:, :, 64:128], ckb(COSK, kt), ALU.mult))
            ve(lambda E, t=T1b, f=FSg, kt=kt: E.tensor_tensor(
                t, f[:, :, 0:64], ckb(SINK, kt), ALU.mult))
            ve(lambda E, o=RO, a=T1a, b=T1b: E.tensor_tensor(
                o[:, :, 64:128], a, b, ALU.add))
            for g in range(4):
                m_t, _ = transpose_to(KT[:, g, kt * 128:(kt + 1) * 128],
                                      RO[:, g, :])
            rope_te[par] = m_t

        wq_gw = [0, 0]
        wq_te = [P.mark("te"), 0]
        q_ps_ve = [0, 0]

        def loadq(ft):
            if ft < 4:
                wdst = WA if ft % 2 == 0 else WPT
                P.wait("gpsimd", "te", wq_te[ft % 2])
                wq_gw[ft % 2] = dma("gA" if ft % 2 == 0 else "gB",
                                    wdst[:, :, :], wqT[ft])
        loadq(0)
        for ft in range(4) if "qloop" not in skip else []:
            wdst = WA if ft % 2 == 0 else WPT
            loadq(ft + 1)
            P.wait("tensor", "gA" if ft % 2 == 0 else "gB", wq_gw[ft % 2])
            for qt in range(NQT):
                par = qt % 2
                pq = PS[4 + par]
                P.wait("tensor", "ve", q_ps_ve[par])
                for ec in range(16):
                    mm = te(lambda E, p=pq, ec=ec, qt=qt, w=wdst: E.matmul(
                        p[:, :], XT[:, ec, qt * 128:(qt + 1) * 128], w[:, ec, :],
                        start=(ec == 0), stop=(ec == 15)))
                wq_te[ft % 2] = mm
                P.wait("vector", "te", mm)
                q_ps_ve[par] = ve(
                    lambda E, p=pq: E.tensor_copy(FS[:, 0:512], p[:, :]))
                FSg = FS[:, 0:512].rearrange("p (g c) -> p g c", c=128)

                def cqb(tab, qt=qt):
                    sl = tab[:, qt, :]
                    return bass.AP(sl.tensor, sl.offset,
                                   [[sl.ap[0][0], 128], [0, 4], [1, 64]])
                T1 = FS2[:, 0:512].rearrange("p (g c) -> p g c", c=64)
                T1a, T1b = T1[:, 0:4, :], T1[:, 4:8, :]
                RO = ROPE_ST[par]
                P.wait("vector", "te", rope_te[par])
                ve(lambda E, t=T1a, f=FSg, qt=qt: E.tensor_tensor(
                    t, f[:, :, 0:64], cqb(COSQ, qt), ALU.mult))
                ve(lambda E, t=T1b, f=FSg, qt=qt: E.tensor_tensor(
                    t, f[:, :, 64:128], cqb(SINQ, qt), ALU.mult))
                ve(lambda E, o=RO, a=T1a, b=T1b: E.tensor_tensor(
                    o[:, :, 0:64], a, b, ALU.subtract))
                ve(lambda E, t=T1a, f=FSg, qt=qt: E.tensor_tensor(
                    t, f[:, :, 64:128], cqb(COSQ, qt), ALU.mult))
                ve(lambda E, t=T1b, f=FSg, qt=qt: E.tensor_tensor(
                    t, f[:, :, 0:64], cqb(SINQ, qt), ALU.mult))
                ve(lambda E, o=RO, a=T1a, b=T1b: E.tensor_tensor(
                    o[:, :, 64:128], a, b, ALU.add))
                for g in range(4):
                    m_t, _ = transpose_to(
                        QT[:, 4 * ft + g, qt * 128:(qt + 1) * 128], RO[:, g, :])
                rope_te[par] = m_t


        P.barrier()
        if "tabs" in taps:
            dma("io", tap_outs["tabs"], AF32.ap()[:, 4096:8192])
        if "qrot" in taps:
            dma("io", tap_outs["qrot"], QT[:, :, :])
        if "krot" in taps:
            dma("io", tap_outs["krot"], KT[:, :, :])
        if "vaug" in taps:
            dma("io", tap_outs["vaug"], VB[:, :, :])
        if upto < 2:
            _emit(nc, P, block)
            return nc

        P.wait("scalar", "io", m_m0b)
        sc_ps_se = [0, 0]
        pv_ps_ve = [0, 0, 0, 0]
        op_ps_ve = [0, 0]
        wo_gw = [0, 0]
        wo_te = [0, 0]
        fs3_ve = [0, 0]
        st_sw = [0, 0]
        for qg in range(2):
            qsl = slice(qg * 512, (qg + 1) * 512)
            for h in range(16):
                g = h // 4
                PTp = PT if h % 2 == 0 else PT2
                for kt in range(16):
                    par = kt % 2
                    ps = PS[par]
                    P.wait("tensor", "se", sc_ps_se[par])
                    P.wait("tensor", "ve", op_ps_ve[par])
                    mm = te(lambda E, p=ps, g=g, kt=kt, h=h, qsl=qsl: E.matmul(
                        p[:, :], KT[:, g, kt * 128:(kt + 1) * 128],
                        QT[:, h, qsl], start=True, stop=True))
                    P.wait("scalar", "te", mm)
                    sc_ps_se[par] = se(lambda E, p=ps, b=PTp, kt=kt: E.activation(
                        out=b[:, kt, :], in_=p[:, :], func=AF.Exp,
                        bias=col(C_M0), scale=1.0))
                m_exp = P.mark("se")
                P.wait("tensor", "se", m_exp)
                for qt in range(4):
                    bank = PS[2 + qt]
                    P.wait("tensor", "ve", pv_ps_ve[qt])
                    for kt in range(16):
                        mm = te(lambda E, p=bank, b=PTp, kt=kt, qt=qt, g=g:
                                E.matmul(
                            p[:, 0:129],
                            b[:, kt, qt * 128:(qt + 1) * 128],
                            VB[:, kt, g * 130:g * 130 + 129],
                            start=(kt == 0), stop=(kt == 15)))
                    P.wait("vector", "te", mm)
                    ve(lambda E, p=bank: E.reciprocal(
                        col(C_RC), p[:, 128:129]))
                    ve(lambda E, p=bank: E.scalar_tensor_tensor(
                        SM_STG[:, 0:128], p[:, 0:128], 1.0,
                        colb(C_RC, 128), ALU.mult, ALU.mult))
                    transpose_to(WB[:, h, qt * 128:(qt + 1) * 128],
                                 SM_STG[:, 0:128])
                    pv_ps_ve[qt] = P.mark("ve")

            def loadwo(of, qg=qg):
                if of < 4:
                    wdst = WA if of % 2 == 0 else WXT
                    P.wait("gpsimd", "te", wo_te[of % 2])
                    wo_gw[of % 2] = dma("gA" if of % 2 == 0 else "gB",
                                        wdst[:, :, :], woT[of])
            loadwo(0)
            for of in range(4):
                wdst = WA if of % 2 == 0 else WXT
                loadwo(of + 1)
                P.wait("tensor", "gA" if of % 2 == 0 else "gB",
                       wo_gw[of % 2])
                for qt in range(4):
                    qtg = qg * 4 + qt
                    par = qt % 2
                    po = PS[par]
                    fs3s = FS3[:, par * 512:(par + 1) * 512]
                    P.wait("sync", "ve", fs3_ve[par])
                    xsem = "xA" if par == 0 else "xB"
                    m_x = dma(xsem, fs3s,
                              x_own[qtg * 128:(qtg + 1) * 128,
                                    of * 512:(of + 1) * 512])
                    P.wait("tensor", "ve", op_ps_ve[par])
                    P.wait("tensor", "se", sc_ps_se[par])
                    for vc in range(16):
                        mm = te(lambda E, p=po, vc=vc, qt=qt, w=wdst: E.matmul(
                            p[:, :], WB[:, vc, qt * 128:(qt + 1) * 128],
                            w[:, vc, :], start=(vc == 0), stop=(vc == 15)))
                    wo_te[of % 2] = mm
                    P.wait("vector", "te", mm)
                    P.wait("vector", xsem, m_x)
                    P.wait("vector", "oA" if par == 0 else "oB", st_sw[par])
                    st = FS2[:, par * 512:(par + 1) * 512]
                    mv = ve(lambda E, p=po, s=st, x=fs3s: E.tensor_add(
                        s, p[:, :], x))
                    op_ps_ve[par] = mv
                    fs3_ve[par] = mv
                    P.wait("sync", "ve", mv)
                    st_sw[par] = dma("oA" if par == 0 else "oB",
                                     sumb[qtg * 128:(qtg + 1) * 128,
                                          of * 512:(of + 1) * 512], st)

        P.barrier()
        if "mstat" in taps:
            dma("io", tap_outs["mstat"], COLS.ap()[:, 0:128])
        if "ctx" in taps:
            dma("io", tap_outs["ctx"], WB[:, :, :])
        if "vaug" in taps and upto == 3:
            dma("io", tap_outs["vaug"], VB[:, :, :])
        if "bsc" in taps:
            dma("io", tap_outs["bsc"], BSC.ap()[:, :])
        if "sumb0" in taps:
            dma("io", tap_outs["sumb0"], sumb[:, :])
        if upto < 3:
            _emit(nc, P, block)
            return nc

        def emit_ln(dst, make_xt, tapname=None):
            m_fs_users = 0
            m_fs2_user = 0
            for qt in range(NQT):
                P.wait("sync", "ve", m_fs2_user)
                m_l = dma("xA", FS2[:, :], sumb[qt * 128:(qt + 1) * 128, :])
                P.wait("vector", "xA", m_l)
                m_v = ve(lambda E: E.reduce_sum(col(C_SUM), FS2[:, :], AX.X))
                P.wait("scalar", "ve", m_v)
                P.wait("scalar", "oA", m_fs_users)
                m_s = se(lambda E: E.activation(out=FS[:, :], in_=FS2[:, :],
                                                func=AF.Square,
                                                accum_out=col(C_SQ)))
                P.wait("vector", "se", m_s)
                col_const(C_T3, 1.0 / H)
                col_tt(C_MEAN, C_SUM, C_T3, ALU.mult)
                col_tt(C_T1, C_SQ, C_T3, ALU.mult)
                col_tt(C_T2, C_MEAN, C_MEAN, ALU.mult)
                col_tt(C_VAR, C_T1, C_T2, ALU.subtract)
                col_const(C_T3, EPS)
                m_v = col_tt(C_VAR, C_VAR, C_T3, ALU.add)
                P.wait("scalar", "ve", m_v)
                m_s = se(lambda E: E.activation(out=col(C_T2), in_=col(C_VAR),
                                                func=AF.Sqrt))
                P.wait("vector", "se", m_s)
                ve(lambda E: E.reciprocal(col(C_RSTD), col(C_T2)))
                col_tt(C_T1, C_RSTD, C_RSTD, ALU.mult)
                col_tt(C_T1, C_T1, C_VAR, ALU.mult)
                col_const(C_T3, -0.5)
                col_tt(C_T1, C_T1, C_T3, ALU.mult)
                col_const(C_T3, 1.5)
                col_tt(C_T1, C_T1, C_T3, ALU.add)
                col_tt(C_RSTD, C_RSTD, C_T1, ALU.mult)
                m_fs2_user = ve(lambda E: E.tensor_scalar(
                    FS[:, :], FS2[:, :], col(C_MEAN), col(C_RSTD),
                    ALU.subtract, ALU.mult))
                P.wait("sync", "ve", m_fs2_user)
                m_fs_users = dma("oA", dst[qt * 128:(qt + 1) * 128, :],
                                 FS[:, :])
                if make_xt:
                    ve(lambda E: E.tensor_copy(CASTB, FS[:, :]))
                    CB = CASTB.rearrange("p (c n) -> p c n", n=128)
                    for c in range(16):
                        transpose_to(XT[:, c, qt * 128:(qt + 1) * 128],
                                     CB[:, c, :])
                    m_fs2_user = P.mark("ve")
            P.barrier()
            if tapname and tapname in taps:
                dma("io", tap_outs[tapname], dst[:, :])

        emit_ln(xm, True, "x1")
        if upto < 4:
            _emit(nc, P, block)
            return nc

        def emit_ffn_a(l):
            gws, uws = (gw0, uw0) if l == 0 else (gw1, uw1)
            w_gw = [0, 0]
            w_te = [0, 0]
            gu_ps = [0, 0, 0, 0]
            gu_sem = ["ve", "ve", "ve", "ve"]
            sc_ve = [0, 0]
            ftw_sw = 0

            def wbufs(it):
                return (WA, WB) if it % 2 == 0 else (QTa, QTb)

            def loadw(it):
                if it < 16:
                    bg, bu = wbufs(it)
                    P.wait("gpsimd", "te", w_te[it % 2])
                    gsem = "gA" if it % 2 == 0 else "gB"
                    dma(gsem, bg[:, :, :], gws[it])
                    w_gw[it % 2] = dma(gsem, bu[:, :, :], uws[it])
            loadw(0)
            for it in range(16):
                bg, bu = wbufs(it)
                loadw(it + 1)
                if l == 0:
                    bsem = "bA" if it % 2 == 0 else "bB"
                    P.wait("sync", "ve", sc_ve[it % 2])
                    bcast_row(FAR[:, (it % 2) * 512:(it % 2) * 512 + 512],
                              sg0[it * 512:it * 512 + 1], 512, sem=bsem)
                    m_sc = bcast_row(
                        FAR[:, 1024 + (it % 2) * 512:1024 + (it % 2) * 512 + 512],
                        su0[it * 512:it * 512 + 1], 512, sem=bsem)
                P.wait("tensor", "gA" if it % 2 == 0 else "gB",
                       w_gw[it % 2])
                if l == 0:
                    P.wait("vector", bsem, m_sc)
                for qt in range(NQT):
                    par = qt % 2
                    pg, pu = PS[par], PS[2 + par]
                    P.wait("tensor", gu_sem[par], gu_ps[par])
                    for ec in range(16):
                        te(lambda E, p=pg, ec=ec, qt=qt, w=bg: E.matmul(
                            p[:, :], XT[:, ec, qt * 128:(qt + 1) * 128],
                            w[:, ec, :], start=(ec == 0), stop=(ec == 15)))
                    P.wait("tensor", gu_sem[2 + par], gu_ps[2 + par])
                    for ec in range(16):
                        mm = te(lambda E, p=pu, ec=ec, qt=qt, w=bu: E.matmul(
                            p[:, :], XT[:, ec, qt * 128:(qt + 1) * 128],
                            w[:, ec, :], start=(ec == 0), stop=(ec == 15)))
                    w_te[it % 2] = mm
                    if l == 0:
                        P.wait("vector", "te", mm)
                        sgs = FAR[:, (it % 2) * 512:(it % 2) * 512 + 512]
                        sus = FAR[:, 1024 + (it % 2) * 512:
                                  1024 + (it % 2) * 512 + 512]
                        mv = ve(lambda E, p=pg, s=sgs: E.tensor_tensor(
                            FS[:, 0:512], p[:, :], s, ALU.mult))
                        gu_ps[par], gu_sem[par] = mv, "ve"
                        sc_ve[it % 2] = mv
                        P.wait("scalar", "ve", mv)
                        m_s = se(lambda E: E.activation(
                            out=FS[:, 512:1024], in_=FS[:, 0:512], func=AF.Silu))
                        mv2 = ve(lambda E, p=pu, s=sus: E.tensor_tensor(
                            FS[:, 1024:1536], p[:, :], s, ALU.mult))
                        gu_ps[2 + par], gu_sem[2 + par] = mv2, "ve"
                        P.wait("vector", "se", m_s)
                        mfu = ve(lambda E: E.tensor_tensor(
                            FUSEB, FS[:, 512:1024], FS[:, 1024:1536], ALU.mult))
                    else:
                        P.wait("scalar", "te", mm)
                        m_s = se(lambda E, p=pg: E.activation(
                            out=FS[:, 0:512], in_=p[:, :], func=AF.Silu))
                        gu_ps[par], gu_sem[par] = m_s, "se"
                        P.wait("vector", "te", mm)
                        mv2 = ve(lambda E, p=pu: E.tensor_copy(FS[:, 512:1024],
                                                               p[:, :]))
                        gu_ps[2 + par], gu_sem[2 + par] = mv2, "ve"
                        P.wait("vector", "se", m_s)
                        mfu = ve(lambda E: E.tensor_tensor(
                            FS2[:, 0:512], FS[:, 0:512], FS[:, 512:1024],
                            ALU.mult))
                        P.wait("scalar", "ve", mfu)
                        m_s = se(lambda E: E.activation(
                            out=FS[:, 1024:1536], in_=FS2[:, 0:512],
                            func=AF.Abs))
                        P.wait("vector", "se", m_s)
                        AV = view(AF32, 1024, (128, 4))
                        for k, (i_, j_) in enumerate(
                                [(0, 1), (0, 2), (0, 3), (1, 2), (1, 3), (2, 3)]):
                            ve(lambda E, k=k, i_=i_, j_=j_: E.tensor_tensor(
                                SM_C6[k], AV[:, :, i_], AV[:, :, j_], ALU.is_ge))
                        ve(lambda E: E.tensor_tensor(SM_TMP, SM_C6[0], SM_C6[1],
                                                     ALU.add))
                        ve(lambda E: E.tensor_tensor(SM_TMP, SM_TMP, SM_C6[2],
                                                     ALU.add))
                        ve(lambda E: E.tensor_single_scalar(SM_K4[0], SM_TMP,
                                                            1.5, ALU.is_ge))
                        ve(lambda E: E.tensor_tensor(SM_TMP, SM_C6[3], SM_C6[4],
                                                     ALU.add))
                        ve(lambda E: E.tensor_tensor(SM_TMP, SM_TMP, SM_C6[0],
                                                     ALU.subtract))
                        ve(lambda E: E.tensor_single_scalar(SM_K4[1], SM_TMP,
                                                            0.5, ALU.is_ge))
                        ve(lambda E: E.tensor_tensor(SM_TMP, SM_C6[5], SM_C6[1],
                                                     ALU.subtract))
                        ve(lambda E: E.tensor_tensor(SM_TMP, SM_TMP, SM_C6[3],
                                                     ALU.subtract))
                        ve(lambda E: E.tensor_single_scalar(SM_K4[2], SM_TMP,
                                                            -0.5, ALU.is_ge))
                        ve(lambda E: E.tensor_tensor(SM_TMP, SM_C6[2], SM_C6[4],
                                                     ALU.add))
                        ve(lambda E: E.tensor_tensor(SM_TMP, SM_TMP, SM_C6[5],
                                                     ALU.add))
                        ve(lambda E: E.tensor_single_scalar(SM_K4[3], SM_TMP,
                                                            1.5, ALU.is_le))
                        FV = view(AF32, 2048, (128, 4))
                        for i_ in range(4):
                            mfu = ve(lambda E, i_=i_: E.tensor_tensor(
                                FUSE4[:, :, i_], FV[:, :, i_], SM_K4[i_],
                                ALU.mult))
                    P.wait("tensor", "ve", mfu)
                    P.wait("vector", "st", ftw_sw)
                    FU = FUSEB.rearrange("p (c n) -> p c n", n=128)
                    for j in range(4):
                        transpose_to(FTW[:, j, qt, :], FU[:, j, :])
                m_all = P.mark("ve")
                P.wait("sync", "ve", m_all)
                for j in range(4):
                    ftw_sw = dma("st", fT[4 * it + j], FTW[:, j, :, :])
            P.barrier()

        def emit_ffn_b(l, dst):
            dws = dw0 if l == 0 else dw1
            strip_gw = [0, 0]
            strip_te = [0, 0]
            fts_te = [0, 0]
            m_fb = [0, 0]
            dn_ps_ve = [0, 0, 0, 0]
            sc_ve = [0, 0]
            fs3_ve2 = [0, 0]
            st_sw2 = [0, 0, 0, 0]

            def strip(of):
                return STRIP0 if of % 2 == 0 else STRIP1

            def loads(of):
                if of < 8:
                    P.wait("gpsimd", "te", strip_te[of % 2])
                    strip_gw[of % 2] = dma("gA" if of % 2 == 0 else "gB",
                                           strip(of)[:, :, :], dws[of])
            loads(0)
            for of in range(8):
                loads(of + 1)
                if l == 0:
                    bsem = "bA" if of % 2 == 0 else "bB"
                    P.wait("sync", "ve", sc_ve[of % 2])
                    m_sc = bcast_row(
                        FAR[:, (of % 2) * 256:(of % 2) * 256 + 256],
                        sd0[of * 256:of * 256 + 1], 256, sem=bsem)
                P.wait("tensor", "gA" if of % 2 == 0 else "gB",
                       strip_gw[of % 2])
                if l == 0:
                    P.wait("vector", bsem, m_sc)
                for qgr in range(2):
                    def loadf(bi, qgr=qgr):
                        if bi >= 16:
                            return
                        p_ = bi % 2
                        P.wait("sync", "te", fts_te[p_])
                        for j in range(4):
                            m_fb[p_] = dma(
                                "fA" if p_ == 0 else "fB",
                                FTS[:, p_ * 4 + j, 0:4, :],
                                fT[bi * 4 + j, :, qgr * 4:(qgr + 1) * 4, :])
                    loadf(0)
                    loadf(1)
                    for bi in range(16):
                        p_ = bi % 2
                        P.wait("tensor", "fA" if p_ == 0 else "fB", m_fb[p_])
                        for j in range(4):
                            ic = bi * 4 + j
                            sl = p_ * 4 + j
                            for qt in range(4):
                                pd = PS[qt]
                                if ic == 0:
                                    P.wait("tensor", "ve", dn_ps_ve[qt])
                                mm = te(lambda E, p=pd, sl=sl, qt=qt, of=of,
                                        ic=ic: E.matmul(
                                    p[:, 0:256], FTS[:, sl, qt, :],
                                    strip(of)[:, ic, :],
                                    start=(ic == 0), stop=(ic == 63)))
                        fts_te[p_] = mm
                        loadf(bi + 2)
                    strip_te[of % 2] = mm
                    P.wait("vector", "te", mm)
                    for qt in range(4):
                        qtg = qgr * 4 + qt
                        pd = PS[qt]
                        par = qt % 2
                        fs3s = FS3[:, par * 256:(par + 1) * 256]
                        P.wait("sync", "ve", fs3_ve2[par])
                        xsem = "xA" if par == 0 else "xB"
                        m_x = dma(xsem, fs3s,
                                  xm[qtg * 128:(qtg + 1) * 128,
                                     of * 256:(of + 1) * 256])
                        st = FS2[:, qt * 256:(qt + 1) * 256]
                        osem = ("oA", "oB", "oC", "oD")[qt]
                        P.wait("vector", xsem, m_x)
                        P.wait("vector", osem, st_sw2[qt])
                        if l == 0:
                            sds = FAR[:, (of % 2) * 256:(of % 2) * 256 + 256]
                            ve(lambda E, p=pd, s=sds: E.tensor_tensor(
                                FS[:, 0:256], p[:, 0:256], s, ALU.mult))
                            mv = ve(lambda E, x=fs3s, s=st: E.tensor_add(
                                s, FS[:, 0:256], x))
                        else:
                            mv = ve(lambda E, p=pd, x=fs3s, s=st: E.tensor_add(
                                s, p[:, 0:256], x))
                        dn_ps_ve[qt] = mv
                        fs3_ve2[par] = mv
                        sc_ve[of % 2] = mv
                        P.wait("sync", "ve", mv)
                        st_sw2[qt] = dma(
                            osem, dst[qtg * 128:(qtg + 1) * 128,
                                      of * 256:(of + 1) * 256], st)
            P.barrier()

        emit_ffn_a(0)
        if upto < 5:
            _emit(nc, P, block)
            return nc
        emit_ffn_b(0, sumb)
        if "sumb1" in taps:
            P.barrier()
            dma("io", tap_outs["sumb1"], sumb[:, :])
        if upto < 6:
            _emit(nc, P, block)
            return nc
        emit_ln(xm, True, "x2")
        if upto < 7:
            _emit(nc, P, block)
            return nc

        caw_gw = [0, 0]
        caw_te = [0, 0]
        qk_ps_se = [0, 0]
        kstg_sw = 0
        vstg_sw = 0
        kag_sw = 0
        vag_sw = 0

        def cawbuf(i):
            return WA if i % 2 == 0 else WPT

        def loadc(i):
            if i < 12:
                P.wait("gpsimd", "te", caw_te[i % 2])
                caw_gw[i % 2] = dma("gA" if i % 2 == 0 else "gB",
                                    cawbuf(i)[:, :, :], caw[i])
        loadc(0)
        for ot in range(8):
            wsrc = cawbuf(ot)
            loadc(ot + 1)
            P.wait("tensor", "gA" if ot % 2 == 0 else "gB",
                   caw_gw[ot % 2])
            for ocl in range(4):
                oc = ot * 4 + ocl
                for ts in range(2):
                    par = (ocl * 2 + ts) % 2
                    pq = PS[par]
                    P.wait("tensor", "ve", qk_ps_se[par])
                    for ec in range(16):
                        mm = te(lambda E, p=pq, ec=ec, ocl=ocl, ts=ts,
                                w=wsrc: E.matmul(
                            p[:, :], w[:, ec, ocl * 128:(ocl + 1) * 128],
                            XT[:, ec, ts * 512:(ts + 1) * 512],
                            start=(ec == 0), stop=(ec == 15)))
                    caw_te[ot % 2] = mm
                    P.wait("vector", "te", mm)
                    if ot < 4:
                        m_s = ve(lambda E, p=pq, oc=oc, ts=ts:
                                 E.scalar_tensor_tensor(
                            QT[:, oc, ts * 512:(ts + 1) * 512], p[:, :], 1.0,
                            bass.AP(CABV.tensor, CABV.offset + oc,
                                    [[CABV.ap[0][0], 128], [0, 512]]),
                            ALU.mult, ALU.add))
                        qk_ps_se[par] = m_s
                    else:
                        ec2 = (ot - 4) * 4 + ocl
                        P.wait("vector", "fA", kstg_sw)
                        m_s = ve(lambda E, p=pq, ec2=ec2:
                                 E.scalar_tensor_tensor(
                            SM_STG, p[:, :], 1.0,
                            bass.AP(CABV.tensor, CABV.offset + 16 + ec2,
                                    [[CABV.ap[0][0], 128], [0, 512]]),
                            ALU.mult, ALU.add))
                        qk_ps_se[par] = m_s
                        ve(lambda E: E.tensor_reduce(
                            col(C_T1), SM_STG, AX.X, ALU.max,
                            apply_absolute_value=True))
                        ve(lambda E: E.tensor_max(col(C_KST), col(C_KST),
                                                  col(C_T1)))
                        if ec2 == 0:
                            ve(lambda E: E.tensor_max(col(C_KCST),
                                                      col(C_KCST), col(C_T1)))
                        P.wait("sync", "ve", m_s)
                        SSG = SM_STG.rearrange("p (a b) -> p a b", b=128)
                        for ktl in range(4):
                            kag_sw = dma("fA",
                                         kag_i[ts * 4 + ktl, :, ec2, :],
                                         SSG[:, ktl, :])
                        kstg_sw = kag_sw
        v_ps_ve = [0, 0]
        m_vb = [0, 0]
        for vt in range(4):
            wsrc = cawbuf(8 + vt)
            loadc(8 + vt + 1)
            m_vb[vt % 2] = bcast_row(
                FAR[:, (vt % 2) * 512:(vt % 2) * 512 + 512],
                cab[4096 + vt * 512:4096 + vt * 512 + 1], 512,
                sem="bA" if vt % 2 == 0 else "bB")
            P.wait("tensor", "gA" if (8 + vt) % 2 == 0 else "gB",
                   caw_gw[(8 + vt) % 2])
            for qt in range(NQT):
                par = qt % 2
                pv = PS[2 + par]
                P.wait("tensor", "ve", v_ps_ve[par])
                for ec in range(16):
                    mm = te(lambda E, p=pv, ec=ec, qt=qt, w=wsrc: E.matmul(
                        p[:, :], XT[:, ec, qt * 128:(qt + 1) * 128],
                        w[:, ec, :], start=(ec == 0), stop=(ec == 15)))
                caw_te[vt % 2] = mm
                P.wait("vector", "te", mm)
                P.wait("vector", "bA" if vt % 2 == 0 else "bB", m_vb[vt % 2])
                P.wait("vector", "fB", vstg_sw)
                fvb = FAR[:, (vt % 2) * 512:(vt % 2) * 512 + 512]
                mv = ve(lambda E, p=pv, b=fvb: E.scalar_tensor_tensor(
                    SM_TT, p[:, :], 1.0, b, ALU.mult, ALU.add))
                v_ps_ve[par] = mv
                P.wait("sync", "ve", mv)
                vag_sw = dma("fB", vag_i[qt, :, vt, :], SM_TT)
                vstg_sw = vag_sw

        P.wait("gpsimd", "fA", kag_sw)
        m_cc1 = P.op("gpsimd", lambda E: E.collective_compute(
            "AllGather", ALU.bypass, replica_groups=rep_groups,
            ins=[kag_i], outs=[kag_o]), inc="cc")
        P.wait("gpsimd", "fB", vag_sw)
        m_cc2 = P.op("gpsimd", lambda E: E.collective_compute(
            "AllGather", ALU.bypass, replica_groups=rep_groups,
            ins=[vag_i], outs=[vag_o]), inc="cc")

        bcast_row(col(C_T1), cw0[0:1], 1)
        m_w1 = bcast_row(col(C_T2), cw1[0:1], 1)
        m_cg = bcast_row(FAR[:, 1024:1152], cgate[0:1], 128)
        P.wait("vector", "io", m_w1)
        col_const(C_T3, 0.5)
        col_tt(C_T2, C_T2, C_T3, ALU.mult)
        col_tt(C_T1, C_T1, C_T2, ALU.add)
        ve(lambda E: E.memset(ROWSCR.ap()[:, 0:64], 1.0))
        mlw = ve(lambda E: E.tensor_scalar(
            ROWSCR.ap()[:, 0:64], ROWSCR.ap()[:, 0:64], col(C_T1), None,
            ALU.mult))
        P.wait("scalar", "ve", mlw)
        mlw = se(lambda E: E.activation(out=ROWSCR.ap()[:, 64:128],
                                        in_=ROWSCR.ap()[:, 0:64],
                                        func=AF.Sigmoid))
        P.wait("vector", "se", mlw)
        ve(lambda E: E.tensor_copy(col(C_LW), ROWSCR.ap()[:, 64:65]))
        col_const(C_QCST, SCL)

        P.barrier()
        if "mstat" in taps:
            dma("io", tap_outs["mstat"], COLS.ap()[:, 0:128])
        if upto < 8:
            _emit(nc, P, block)
            return nc

        P.wait("sync", "cc", m_cc1)
        for kt in range(16):
            r, ktl = kt // 8, kt % 8
            m_kcam = dma("io", KCAM[:, kt, :], kag_o[r, ktl, :, 0, :])
        P.wait("sync", "cc", m_cc2)
        for kt in range(16):
            r, ktl = kt // 8, kt % 8
            m_vcam = dma("vA", VCAM[:, kt, 0:128],
                         vag_o[r, ktl, :, 0, 0:128])
        mves = ve(lambda E: E.memset(VCAM[:, :, 128:129], 1.0))
        if "kcam" in taps:
            P.wait("sync", "io", P.mark("io"))
            P.wait("sync", "vA", P.mark("vA"))
            dma("io", tap_outs["kcam"], KCAM[:, :, :])
            dma("io", tap_outs["qcam"], QT[:, 0, :])
            dma("io", tap_outs["vcam"], VCAM[:, :, 0:128])

        sc_se2 = [0, 0]
        campv_ve = [0, 0]
        pv_bank_ve = [0, 0, 0, 0]
        stg_te = 0
        cpw_gw = [0, 0]
        cpw_te = [0, 0]
        op_ve2 = [0, 0]
        fs3_ve3 = [0, 0]
        st_sw3 = [0, 0]
        for qg in range(2):
            qsl = slice(qg * 512, (qg + 1) * 512)
            for kt in range(16):
                par = kt % 2
                ps = PS[par]
                P.wait("tensor", "io", m_kcam)
                P.wait("tensor", "se", sc_se2[par])
                P.wait("tensor", "ve", op_ve2[par])
                mm = te(lambda E, p=ps, kt=kt, qsl=qsl: E.matmul(
                    p[:, :], KCAM[:, kt, :], QT[:, 0, qsl],
                    start=True, stop=True))
                P.wait("scalar", "te", mm)
                sc_se2[par] = se(lambda E, p=ps, kt=kt: E.activation(
                    out=PT2[:, kt, :], in_=p[:, :], func=AF.Exp,
                    bias=col(C_M0), scale=col(C_QCST)))
            m_exp = P.mark("se")
            P.wait("tensor", "se", m_exp)
            P.wait("tensor", "ve", mves)
            P.wait("tensor", "vA", m_vcam)
            for qt in range(4):
                bank = PS[2 + qt // 2]
                base = (qt % 2) * 129
                if qt % 2 == 0:
                    P.wait("tensor", "ve", campv_ve[qt // 2])
                for kt in range(16):
                    mm = te(lambda E, p=bank, kt=kt, qt=qt, base=base: E.matmul(
                        p[:, base:base + 129],
                        PT2[:, kt, qt * 128:(qt + 1) * 128],
                        VCAM[:, kt, 0:129], start=(kt == 0), stop=(kt == 15)))
                if qt % 2 == 1:
                    P.wait("vector", "te", mm)
                    for q2 in (qt - 1, qt):
                        b2 = (q2 % 2) * 129
                        ve(lambda E, p=bank, b2=b2: E.reciprocal(
                            col(C_RC), p[:, b2 + 128:b2 + 129]))
                        ve(lambda E, p=bank, b2=b2: E.scalar_tensor_tensor(
                            FS[:, 0:128], p[:, b2:b2 + 128], 1.0,
                            colb(C_RC, 128), ALU.mult, ALU.mult))
                        P.wait("vector", "io", m_cg)
                        m_v = ve(lambda E: E.tensor_tensor(
                            FS[:, 128:256], FS[:, 0:128], FAR[:, 1024:1152],
                            ALU.mult))
                        P.wait("scalar", "ve", m_v)
                        m_s = se(lambda E: E.activation(
                            out=FS[:, 256:384], in_=FS[:, 128:256],
                            func=AF.Tanh))
                        P.wait("vector", "se", m_s)
                        ve(lambda E: E.tensor_scalar(
                            SM_T, FS[:, 256:384], col(C_LW), None, ALU.mult))
                        transpose_to(SM_TT[:, q2 * 128:(q2 + 1) * 128], SM_T)
                    campv_ve[qt // 2] = P.mark("ve")
            m_tt = P.mark("ve")
            if "camT" in taps and qg == 1:
                P.wait("sync", "ve", m_tt)
                dma("io", tap_outs["camT"], SM_TT[:, :])
            if "camf" in taps and qg == 1:
                P.wait("sync", "ve", m_tt)
                dma("io", tap_outs["camf"], FS[:, 0:384])
            kts_te = [0, 0]
            P.wait("tensor", "ve", m_tt)
            for kt in range(16):
                par = kt % 2
                r, ktl = kt // 8, kt % 8
                ps = PS[par]
                P.wait("sync", "te", kts_te[par])
                ksem = "fA" if par == 0 else "fB"
                m_k = dma(ksem, KTS[par][:, :, :], kag_o[r, ktl])
                P.wait("tensor", ksem, m_k)
                P.wait("tensor", "se", sc_se2[par])
                for ec in range(16):
                    te(lambda E, p=ps, par=par, ec=ec, qsl=qsl: E.matmul(
                        p[:, :], KTS[par][:, ec, :], QT[:, ec, qsl],
                        start=(ec == 0), stop=False))
                mm = te(lambda E, p=ps, kt=kt: E.matmul(
                    p[:, :], KCAM[:, kt, :], SM_TT, start=False, stop=True))
                kts_te[par] = mm
                P.wait("scalar", "te", mm)
                sc_se2[par] = se(lambda E, p=ps, kt=kt: E.activation(
                    out=PT[:, kt, :], in_=p[:, :], func=AF.Exp,
                    bias=col(C_M0), scale=col(C_QCST)))
            m_exp = P.mark("se")
            P.wait("tensor", "se", m_exp)
            P.wait("tensor", "ve", campv_ve[0])
            for qt in range(4):
                for kt in range(16):
                    mm = te(lambda E, qt=qt, kt=kt: E.matmul(
                        PS[2][:, qt:qt + 1],
                        PT[:, kt, qt * 128:(qt + 1) * 128],
                        ONES[:, 0:1], start=(kt == 0), stop=(kt == 15)))
            P.wait("vector", "te", mm)
            m_rc = ve(lambda E: E.reciprocal(COLS.ap()[:, C_RC4:C_RC4 + 4],
                                             PS[2][:, 0:4]))
            vst_te = 0
            for of in range(4):
                P.wait("sync", "te", vst_te)
                for kt in range(16):
                    r, ktl = kt // 8, kt % 8
                    m_vs = dma("vA", VBS[:, kt, :],
                               vag_o[r, ktl, :, of, :])
                for qt in range(4):
                    pvb = PS[2 + qt]
                    P.wait("tensor", "ve", pv_bank_ve[qt])
                    if of == 0 and qt == 0:
                        P.wait("tensor", "ve", m_rc)
                    P.wait("tensor", "vA", m_vs)
                    for kt in range(16):
                        mm = te(lambda E, p=pvb, kt=kt, qt=qt: E.matmul(
                            p[:, :], PT[:, kt, qt * 128:(qt + 1) * 128],
                            VBS[:, kt, :], start=(kt == 0), stop=(kt == 15)))
                    vst_te = mm
                    P.wait("vector", "te", mm)
                    mv = ve(lambda E, p=pvb, qt=qt: E.scalar_tensor_tensor(
                        SM_STG, p[:, :], 1.0,
                        bass.AP(COLS.ap().tensor,
                                COLS.ap().offset + C_RC4 + qt,
                                [[COLS.ap().ap[0][0], 128], [0, 512]]),
                        ALU.mult, ALU.mult))
                    pv_bank_ve[qt] = mv
                    last_t = 0
                    for j in range(4):
                        m_t, m_v2 = transpose_to(
                            PT2[:, of * 4 + j, qt * 128:(qt + 1) * 128],
                            SM_STG4[:, j, :])
                        last_t = m_v2
                    stg_te = last_t
            if "ctx1" in taps and qg == 1:
                P.wait("sync", "ve", P.mark("ve"))
                dma("io", tap_outs["ctx1"], PT2[:, :, :])
            def loadcp(of):
                if of < 4:
                    wdst = WA if of % 2 == 0 else WXT
                    P.wait("gpsimd", "te", cpw_te[of % 2])
                    cpw_gw[of % 2] = dma("gA" if of % 2 == 0 else "gB",
                                         wdst[:, :, :], cpw[of])
            loadcp(0)
            for of in range(4):
                wdst = WA if of % 2 == 0 else WXT
                loadcp(of + 1)
                bsem = "bA" if of % 2 == 0 else "bB"
                m_cb = bcast_row(FAR[:, (of % 2) * 512:(of % 2) * 512 + 512],
                                 cpb[of * 512:of * 512 + 1], 512, sem=bsem)
                P.wait("tensor", "gA" if of % 2 == 0 else "gB",
                       cpw_gw[of % 2])
                for qt in range(4):
                    qtg = qg * 4 + qt
                    par = qt % 2
                    po = PS[par]
                    fs3s = FS3[:, par * 512:(par + 1) * 512]
                    P.wait("sync", "ve", fs3_ve3[par])
                    xsem = "xA" if par == 0 else "xB"
                    m_x = dma(xsem, fs3s, xm[qtg * 128:(qtg + 1) * 128,
                                             of * 512:(of + 1) * 512])
                    P.wait("tensor", "ve", op_ve2[par])
                    for vc in range(16):
                        mm = te(lambda E, p=po, vc=vc, qt=qt, w=wdst: E.matmul(
                            p[:, :], PT2[:, vc, qt * 128:(qt + 1) * 128],
                            w[:, vc, :], start=(vc == 0), stop=(vc == 15)))
                    cpw_te[of % 2] = mm
                    P.wait("vector", "te", mm)
                    P.wait("vector", xsem, m_x)
                    P.wait("vector", bsem, m_cb)
                    P.wait("vector", "oA" if par == 0 else "oB", st_sw3[par])
                    st = FS2[:, par * 512:(par + 1) * 512]
                    cbs = FAR[:, (of % 2) * 512:(of % 2) * 512 + 512]
                    ve(lambda E, p=po, b=cbs, s=st: E.scalar_tensor_tensor(
                        s, p[:, :], 1.0, b, ALU.mult, ALU.add))
                    mv = ve(lambda E, s=st, x=fs3s: E.tensor_add(s, s, x))
                    op_ve2[par] = mv
                    fs3_ve3[par] = mv
                    P.wait("sync", "ve", mv)
                    st_sw3[par] = dma(
                        "oA" if par == 0 else "oB",
                        sumb[qtg * 128:(qtg + 1) * 128,
                             of * 512:(of + 1) * 512], st)
        P.barrier()
        if "sumb2" in taps:
            dma("io", tap_outs["sumb2"], sumb[:, :])
        if upto < 9:
            _emit(nc, P, block)
            return nc

        emit_ln(xm, True, "x3")
        if upto < 10:
            _emit(nc, P, block)
            return nc
        emit_ffn_a(1)
        if upto < 11:
            _emit(nc, P, block)
            return nc
        emit_ffn_b(1, sumb)
        if upto < 12:
            _emit(nc, P, block)
            return nc
        emit_ln(y, False, None)

        _emit(nc, P, block)
    return nc


def _emit(nc, P, block):
    for snm in ("io", "bA", "bB", "xA", "xB", "fA", "fB", "vA", "st",
                "oA", "oB", "oC", "oD"):
        P.wait("sync", snm, P.mark(snm))
    for snm in ("gA", "gB", "sA", "sB"):
        P.wait("gpsimd", snm, P.mark(snm))
    P.barrier()

    @block.tensor
    def _(E):
        for f in P.sched["tensor"]:
            f(E)

    @block.vector
    def _(E):
        for f in P.sched["vector"]:
            f(E)

    @block.scalar
    def _(E):
        for f in P.sched["scalar"]:
            f(E)

    @block.gpsimd
    def _(E):
        for f in P.sched["gpsimd"]:
            f(E)

    @block.sync
    def _(E):
        for f in P.sched["sync"]:
            f(E)


def kernel(**inputs):
    maps = stage_inputs(inputs)
    nc = build_nc()
    res = run_bass_kernel_spmd(nc, maps, core_ids=list(range(8)))
    return np.concatenate(
        [r["y"] for r in res.results], axis=0).reshape(4, 2048, 2048)
